# version 04 (final)
# baseline (speedup 1.0000x reference)
import numpy as np
from contextlib import ExitStack

import concourse.bass as bass
import concourse.mybir as mybir
from concourse.bass_utils import run_bass_kernel_spmd

F32 = mybir.dt.float32
BF16 = mybir.dt.bfloat16
AF = mybir.ActivationFunctionType
ALU = mybir.AluOpType
AX = mybir.AxisListType

B, S, D, H, E = 4, 4096, 1024, 4096, 8
T = 2048
DC = D // 128
NT = S // 128
QT = T // 128
EPS_LN = 1e-5
EPS_QK = 1e-12
INV_D = 1.0 / D


class Sem:
    def __init__(self, h):
        self.h = h
        self.n = 0


def wl(*evs):
    out = []
    for e in evs:
        if e is None:
            continue
        if isinstance(e, list):
            out.extend(x for x in e if x is not None)
        else:
            out.append(e)
    return out


def build_program(debug=False, last_phase=6, b1_tiles=QT, b1_stage=5, s2_stage=4):
    nc = bass.Bass("TRN2", target_bir_lowering=False)

    xs = nc.dram_tensor("xs", [S, D], F32, kind="ExternalInput")
    xq = nc.dram_tensor("xq", [T, D], F32, kind="ExternalInput")
    wq = nc.dram_tensor("wq", [D, D], F32, kind="ExternalInput")
    wk = nc.dram_tensor("wk", [D, D], F32, kind="ExternalInput")
    wv = nc.dram_tensor("wv", [D, D], F32, kind="ExternalInput")
    w1 = nc.dram_tensor("w1", [D, D], F32, kind="ExternalInput")
    w2 = nc.dram_tensor("w2", [8, D, 512], F32, kind="ExternalInput")
    w3 = nc.dram_tensor("w3", [8, 512, D], F32, kind="ExternalInput")
    bq = nc.dram_tensor("bq", [D], F32, kind="ExternalInput")
    bk = nc.dram_tensor("bk", [D], F32, kind="ExternalInput")
    bv = nc.dram_tensor("bv", [D], F32, kind="ExternalInput")
    b1 = nc.dram_tensor("b1", [D], F32, kind="ExternalInput")
    b2 = nc.dram_tensor("b2", [H], F32, kind="ExternalInput")
    b3 = nc.dram_tensor("b3", [D], F32, kind="ExternalInput")
    wg = nc.dram_tensor("wg", [E, 4, D, 1024], F32, kind="ExternalInput")
    wu = nc.dram_tensor("wu", [E, 4, D, 1024], F32, kind="ExternalInput")
    wd = nc.dram_tensor("wd", [E, 4, 1024, D], F32, kind="ExternalInput")
    y = nc.dram_tensor("y", [T, D], F32, kind="ExternalOutput")
    acc_s = nc.dram_tensor("acc_s", [T, D], F32,
                           kind="ExternalOutput" if debug else "Internal")
    dbg = {}
    if debug:
        dbg["kfm"] = nc.dram_tensor("dbg_kfm", [128, DC * S], BF16, kind="ExternalOutput")
        dbg["vkm"] = nc.dram_tensor("dbg_vkm", [128, NT * D], BF16, kind="ExternalOutput")
        dbg["av"] = nc.dram_tensor("dbg_av", [128, QT * D], BF16, kind="ExternalOutput")
        dbg["t1"] = nc.dram_tensor("dbg_t1", [128, 8 * T], BF16, kind="ExternalOutput")
        dbg["t3"] = nc.dram_tensor("dbg_t3", [128, QT * D], F32, kind="ExternalOutput")
        dbg["yfm"] = nc.dram_tensor("dbg_yfm", [128, DC * T], BF16, kind="ExternalOutput")

    es = ExitStack()
    with es:
        def sb(name, shape, dtype, side="left", stack=None):
            return (stack or es).enter_context(
                nc.sbuf_tensor(name, shape, dtype, side=side))

        s_x = [Sem(es.enter_context(nc.semaphore("s_x0"))),
               Sem(es.enter_context(nc.semaphore("s_x1")))]
        s_w = [Sem(es.enter_context(nc.semaphore("s_w0"))),
               Sem(es.enter_context(nc.semaphore("s_w1")))]
        s_fl = [Sem(es.enter_context(nc.semaphore("s_fl0"))),
                Sem(es.enter_context(nc.semaphore("s_fl1")))]
        s_st = [Sem(es.enter_context(nc.semaphore("s_st0"))),
                Sem(es.enter_context(nc.semaphore("s_st1")))]
        s_mm = Sem(es.enter_context(nc.semaphore("s_mm")))
        s_ve = Sem(es.enter_context(nc.semaphore("s_ve")))
        s_se = Sem(es.enter_context(nc.semaphore("s_se")))
        s_out = Sem(es.enter_context(nc.semaphore("s_out")))

        xbuf = [sb("xbuf0", [128, D], F32), sb("xbuf1", [128, D], F32)]
        xnb = [sb("xnb0", [128, D], BF16), sb("xnb1", [128, D], BF16)]
        xnf = [sb("xnf0", [128, D], BF16), sb("xnf1", [128, D], BF16)]
        eye = sb("eye", [128, 128], BF16)
        ones = sb("ones", [1, 128], BF16)
        stats = sb("stats", [128, 64], F32)
        mxc = sb("mxc", [128, 8], F32)

        def st(slot, j):
            c = (slot % 4) * 16 + j
            return stats[:, c:c + 1]

        ENGS = ("t", "v", "s", "g", "y")
        streams = None

        def emit(eng, waits, build, inc=None, amt=1):
            if inc is not None:
                inc.n += amt
            def f(e, waits=tuple(waits), build=build, inc=inc, amt=amt):
                for (sm, val) in waits:
                    if val > 0:
                        e.wait_ge(sm.h, val)
                ins = build(e)
                if inc is not None:
                    ins.then_inc(inc.h, amt)
            streams[eng].append(f)
            return None if inc is None else (inc, inc.n)

        def run_block():
            with nc.Block() as blk:
                if streams["t"]:
                    blk.tensor(lambda e, fs=tuple(streams["t"]): [f(e) for f in fs])
                if streams["v"]:
                    blk.vector(lambda e, fs=tuple(streams["v"]): [f(e) for f in fs])
                if streams["s"]:
                    blk.scalar(lambda e, fs=tuple(streams["s"]): [f(e) for f in fs])
                if streams["g"]:
                    blk.gpsimd(lambda e, fs=tuple(streams["g"]): [f(e) for f in fs])
                if streams["y"]:
                    blk.sync(lambda e, fs=tuple(streams["y"]): [f(e) for f in fs])

        def new_streams():
            nonlocal streams
            streams = {k: [] for k in ENGS}

        def dma_cast_w(dram, e_off, nblk, inner, dest, sem=None):
            ap_in = bass.AP(dram, e_off, [[inner, 128], [128 * inner, nblk], [1, inner]])
            return emit("g", [], lambda g, ap_in=ap_in, dest=dest:
                        g.dma_start(dest, ap_in), inc=(sem or s_w[0]), amt=16)

        def sp(eng):
            emit(eng, [], lambda e: e.nop(cycle_cnt=256))

        def ln_stats(slot, src_f32, junk, waits_v, waits_s):
            ev_sq = emit("s", waits_s, lambda s, a=src_f32, c=st(slot, 1), junk=junk:
                         s.activation(junk, a, AF.Square, accum_out=c), inc=s_se)
            ev_sum = emit("v", waits_v, lambda v, a=src_f32, c=st(slot, 0):
                          v.tensor_reduce(c, a, AX.XYZW, ALU.add), inc=s_ve)
            ev_m = emit("v", [ev_sum], lambda v, ci=st(slot, 0), co=st(slot, 2):
                        v.tensor_scalar(out=co, in0=ci, scalar1=INV_D, scalar2=None,
                                        op0=ALU.mult), inc=s_ve)
            emit("v", [ev_m], lambda v, ci=st(slot, 2), co=st(slot, 4):
                 v.tensor_tensor(co, ci, ci, ALU.mult))
            ev_v1 = emit("v", [ev_sq], lambda v, ci=st(slot, 1), co=st(slot, 3):
                         v.tensor_scalar(out=co, in0=ci, scalar1=INV_D, scalar2=EPS_LN,
                                         op0=ALU.mult, op1=ALU.add), inc=s_ve)
            ev_var = emit("v", [ev_v1], lambda v, a=st(slot, 3), b=st(slot, 4),
                          co=st(slot, 5):
                          v.tensor_tensor(co, a, b, ALU.subtract), inc=s_ve)
            ev_ln = emit("s", [ev_var], lambda s, ci=st(slot, 5), co=st(slot, 6):
                         s.activation(co, ci, AF.Ln), inc=s_se)
            return emit("s", [ev_ln], lambda s, ci=st(slot, 6), co=st(slot, 7):
                        s.activation(co, ci, AF.Exp, scale=-0.5), inc=s_se)

        def ln_apply(slot, src_f32, dst_bf16, waits):
            return emit("v", waits, lambda v, a=src_f32, o=dst_bf16, m=st(slot, 2),
                        r=st(slot, 7):
                        v.tensor_scalar(out=o, in0=a, scalar1=m, scalar2=r,
                                        op0=ALU.subtract, op1=ALU.mult), inc=s_ve)

        def transposes8(src_bf16, pbank, waits):
            def bld(t, src=src_bf16, pbank=pbank):
                for dc in range(DC):
                    ins = t.transpose(pbank[:, dc * 128:(dc + 1) * 128],
                                      src[:, dc * 128:(dc + 1) * 128], eye[:, :])
                return ins
            return emit("t", waits, bld, inc=s_mm)

        def proj_mm(pdst, xnf_t, w_fm, brow, waits):
            def bld(t, pdst=pdst, xnf_t=xnf_t, w_fm=w_fm, brow=brow):
                for hh in range(2):
                    for dc in range(DC):
                        t.matmul(pdst[:, hh * 512:(hh + 1) * 512],
                                 xnf_t[:, dc * 128:(dc + 1) * 128],
                                 w_fm[:, dc * D + hh * 512: dc * D + (hh + 1) * 512],
                                 start=(dc == 0), stop=False)
                    ins = t.matmul(pdst[:, hh * 512:(hh + 1) * 512],
                                   ones[0:1, :], brow[0:1, hh * 512:(hh + 1) * 512],
                                   start=False, stop=True)
                return ins
            return emit("t", waits, bld, inc=s_mm)

        def l2_norm(slot, psrc, junk, waits):
            ev_sq = emit("s", waits, lambda s, a=psrc, c=st(slot, 8), junk=junk:
                         s.activation(junk, a, AF.Square, accum_out=c), inc=s_se)
            ev_ln = emit("s", [ev_sq], lambda s, ci=st(slot, 8), co=st(slot, 9):
                         s.activation(co, ci, AF.Ln), inc=s_se)
            ev_n = emit("s", [ev_ln], lambda s, ci=st(slot, 9), co=st(slot, 10):
                        s.activation(co, ci, AF.Exp, scale=0.5), inc=s_se)
            ev_mx = emit("v", [ev_n], lambda v, ci=st(slot, 10), co=st(slot, 11):
                         v.tensor_scalar(out=co, in0=ci, scalar1=EPS_QK, scalar2=None,
                                         op0=ALU.max), inc=s_ve)
            return emit("v", [ev_mx], lambda v, ci=st(slot, 11), co=st(slot, 12):
                        v.reciprocal(co, ci), inc=s_ve)

        esKV = ExitStack()
        K_fm = sb("K_fm", [128, DC * S], BF16, stack=esKV)
        V_km = sb("V_km", [128, NT * D], BF16, stack=esKV)
        junkA = V_km[:, (NT - 1) * D:]

        esA = ExitStack()
        wk_fm = sb("wk_fm", [128, DC * D], BF16, side="right", stack=esA)
        wv_fm = sb("wv_fm", [128, DC * D], BF16, side="right", stack=esA)
        bk_row = sb("bk_row", [1, D], BF16, side="right", stack=esA)
        bv_row = sb("bv_row", [1, D], BF16, side="right", stack=esA)
        knb = sb("knb", [128, D], BF16, side="right", stack=esA)
        pK = esA.enter_context(nc.psum_tensor("pK", [128, D], F32))
        pV = esA.enter_context(nc.psum_tensor("pV", [128, D], F32))
        pT = [esA.enter_context(nc.psum_tensor("pT0", [128, 1024], BF16)),
              esA.enter_context(nc.psum_tensor("pT1", [128, 1024], BF16))]
        pT2 = esA.enter_context(nc.psum_tensor("pT2", [128, 1024], BF16))

        new_streams()
        emit("g", [], lambda g: g.memset(ones[:, :], 1.0))
        emit("g", [], lambda g: g.iota(xbuf[0][:, 0:128], [[1, 128]],
                                       channel_multiplier=0,
                                       allow_small_or_imprecise_dtypes=True))
        ev = emit("g", [], lambda g: g.iota(stats[:, 63:64], [[1, 1]],
                                            channel_multiplier=1,
                                            allow_small_or_imprecise_dtypes=True),
                  inc=s_w[1])
        ev_eye = emit("v", [ev], lambda v:
                      v.tensor_scalar(out=eye[:, :], in0=xbuf[0][:, 0:128],
                                      scalar1=stats[:, 63:64], scalar2=None,
                                      op0=ALU.is_equal), inc=s_ve)
        dma_cast_w(wk, 0, DC, D, wk_fm[:, :])
        dma_cast_w(wv, 0, DC, D, wv_fm[:, :])
        emit("g", [], lambda g: g.dma_start(bk_row[:, :], bass.AP(bk, 0, [[1, 1], [1, D]])),
             inc=s_w[0], amt=16)
        ev_wA = emit("g", [], lambda g:
                     g.dma_start(bv_row[:, :], bass.AP(bv, 0, [[1, 1], [1, D]])),
                     inc=s_w[0], amt=16)

        x_free = [None, None]
        xnb_free = [None, None]
        xnf_free = [None, None]
        knb_free = None
        pT_free = [None, None]
        pT2_free = None
        pK_free = None
        pV_free = None
        slot_free = [None] * 4

        for t_i in range(NT):
            i = t_i % 2
            slot = t_i % 4
            ev_x = emit("y", wl(x_free[i], ev_eye if t_i == 0 else None),
                        lambda sy, t_i=t_i, i=i:
                        sy.dma_start(xbuf[i][:, :], xs[t_i * 128:(t_i + 1) * 128, :]),
                        inc=s_x[i], amt=16)
            ev_rstd = ln_stats(slot, xbuf[i][:, :], junkA,
                               wl(ev_x, slot_free[slot]),
                               wl(ev_x, slot_free[slot]))
            ev_ap = ln_apply(slot, xbuf[i][:, :], xnb[i][:, :],
                             wl(ev_rstd, xnb_free[i]))
            x_free[i] = [ev_ap, ev_rstd]
            ev_tp = transposes8(xnb[i][:, :], pT[i], wl(ev_ap, pT_free[i], ev_eye))
            xnb_free[i] = ev_tp
            ev_xnf = emit("v", wl(ev_tp, xnf_free[i]),
                          lambda v, i=i: v.tensor_copy(xnf[i][:, :], pT[i][:, :]),
                          inc=s_ve)
            pT_free[i] = ev_xnf
            ev_k = proj_mm(pK, xnf[i], wk_fm, bk_row, wl(ev_xnf, ev_wA, pK_free))
            ev_v = proj_mm(pV, xnf[i], wv_fm, bv_row, wl(ev_xnf, ev_wA, pV_free))
            xnf_free[i] = ev_v
            ev_rinv = l2_norm(slot, pK[:, :], junkA, wl(ev_k))
            ev_kn = emit("s", wl(ev_rinv, knb_free),
                         lambda s, r=st(slot, 12):
                         s.activation(knb[:, :], pK[:, :], AF.Copy, scale=r),
                         inc=s_se)
            pK_free = ev_kn
            ev_ve = emit("s", wl(ev_v), lambda s, t_i=t_i:
                         s.activation(V_km[:, t_i * D:(t_i + 1) * D], pV[:, :], AF.Copy),
                         inc=s_se)
            pV_free = ev_ve
            ev_kt = transposes8(knb[:, :], pT2, wl(ev_kn, pT2_free))
            knb_free = ev_kt
            out_ap = bass.AP(K_fm, t_i * 128, [[DC * S, 128], [S, DC], [1, 128]])
            ev_kf = emit("v", wl(ev_kt), lambda v, out_ap=out_ap:
                         v.tensor_copy(out_ap, pT2[:, :]), inc=s_ve)
            pT2_free = ev_kf
            slot_free[slot] = [ev_kf, ev_ve]

        if debug:
            emit("y", [(s_ve, s_ve.n), (s_se, s_se.n)], lambda sy:
                 sy.dma_start(dbg["kfm"][:, :], K_fm[:, :]), inc=s_out, amt=16)
            emit("y", [], lambda sy:
                 sy.dma_start(dbg["vkm"][:, :], V_km[:, :]), inc=s_out, amt=16)
            emit("y", [(s_out, s_out.n)], lambda sy: sy.nop())
        run_block()
        esA.close()
        if last_phase <= 1:
            esKV.close()
            return nc

        esAV = ExitStack()
        AV_all = sb("AV_all", [128, QT * D], BF16, side="right", stack=esAV)

        esB = ExitStack()
        wq_fm = sb("wq_fm", [128, DC * D], BF16, side="right", stack=esB)
        bq_row = sb("bq_row", [1, D], BF16, side="right", stack=esB)
        sim_bf = sb("sim_bf", [128, S], BF16, side="right", stack=esB)
        attn_T = [sb("attn_T0", [128, 1024], BF16, side="right", stack=esB),
                  sb("attn_T1", [128, 1024], BF16, side="right", stack=esB)]
        junkB = sim_bf[:, 0:1024]
        pQ = esB.enter_context(nc.psum_tensor("pQ", [128, D], F32))
        pS = [esB.enter_context(nc.psum_tensor("pS0", [128, 512], F32)),
              esB.enter_context(nc.psum_tensor("pS1", [128, 512], F32))]
        pT = [esB.enter_context(nc.psum_tensor("pTb0", [128, 1024], BF16)),
              esB.enter_context(nc.psum_tensor("pTb1", [128, 1024], BF16))]
        pAV = esB.enter_context(nc.psum_tensor("pAV", [128, D], F32))

        new_streams()
        dma_cast_w(wq, 0, DC, D, wq_fm[:, :])
        ev_wq = emit("g", [], lambda g:
                     g.dma_start(bq_row[:, :], bass.AP(bq, 0, [[1, 1], [1, D]])),
                     inc=s_w[0], amt=16)

        x_free = [None, None]
        xnb_free = [None, None]
        xnf_free = [None, None]
        pT_free = [None, None]
        pS_free = [None, None]
        pQ_free = None
        pAV_free = None
        sim_cons = None
        at_free = [None, None]
        slot_free = [None] * 4

        for q_i in range(b1_tiles):
            i = q_i % 2
            slot = q_i % 4
            ev_x = emit("y", wl(x_free[i]), lambda sy, q_i=q_i, i=i:
                        sy.dma_start(xbuf[i][:, :], xq[q_i * 128:(q_i + 1) * 128, :]),
                        inc=s_x[i], amt=16)
            ev_rstd = ln_stats(slot, xbuf[i][:, :], junkB,
                               wl(ev_x, slot_free[slot]),
                               wl(ev_x, slot_free[slot], sim_cons))
            ev_ap = ln_apply(slot, xbuf[i][:, :], xnb[i][:, :],
                             wl(ev_rstd, xnb_free[i]))
            x_free[i] = [ev_ap, ev_rstd]
            ev_tp = transposes8(xnb[i][:, :], pT[i], wl(ev_ap, pT_free[i]))
            xnb_free[i] = ev_tp
            ev_xnf = emit("v", wl(ev_tp, xnf_free[i]),
                          lambda v, i=i: v.tensor_copy(xnf[i][:, :], pT[i][:, :]),
                          inc=s_ve)
            pT_free[i] = ev_xnf
            if b1_stage < 2:
                slot_free[slot] = ev_xnf
                continue
            ev_q = proj_mm(pQ, xnf[i], wq_fm, bq_row, wl(ev_xnf, ev_wq, pQ_free))
            ev_rinv = l2_norm(slot, pQ[:, :], junkB, wl(ev_q))
            ev_qn = emit("s", wl(ev_rinv),
                         lambda s, i=i, r=st(slot, 12):
                         s.activation(xnb[i][:, :], pQ[:, :], AF.Copy, scale=r),
                         inc=s_se)
            pQ_free = ev_qn
            ev_qt = transposes8(xnb[i][:, :], pT[i], wl(ev_qn, pT_free[i]))
            xnb_free[i] = ev_qt
            ev_qf = emit("v", wl(ev_qt), lambda v, i=i:
                         v.tensor_copy(xnf[i][:, :], pT[i][:, :]), inc=s_ve)
            pT_free[i] = ev_qf
            qn = xnf[i]
            if b1_stage < 3:
                slot_free[slot] = ev_qf
                continue

            sim_evac_last = None
            for kc in range(8):
                j = kc % 2
                def bldS(t, j=j, kc=kc, qn=qn):
                    for dc in range(DC):
                        ins = t.matmul(pS[j][:, :], qn[:, dc * 128:(dc + 1) * 128],
                                       K_fm[:, dc * S + kc * 512: dc * S + (kc + 1) * 512],
                                       start=(dc == 0), stop=(dc == DC - 1))
                    return ins
                ev_s = emit("t", wl(ev_qf, pS_free[j]), bldS, inc=s_mm)
                if b1_stage == 31:
                    pS_free[j] = ev_s
                    sim_evac_last = ev_s
                    continue
                ev_ev = emit("s", wl(ev_s), lambda s, j=j, kc=kc:
                             s.activation(sim_bf[:, kc * 512:(kc + 1) * 512],
                                          pS[j][:, :], AF.Copy), inc=s_se)
                sim_evac_last = ev_ev
                if b1_stage == 32:
                    pS_free[j] = ev_ev
                    continue
                ev_mx = emit("v", wl(ev_ev), lambda v, j=j, kc=kc:
                             v.tensor_reduce(mxc[:, kc:kc + 1], pS[j][:, :],
                                             AX.XYZW, ALU.max), inc=s_ve)
                pS_free[j] = [ev_ev, ev_mx]
            sim_cons = sim_evac_last
            if b1_stage < 4 or b1_stage > 30:
                slot_free[slot] = wl(sim_evac_last, pS_free[0], pS_free[1])
                continue
            ev_nm = emit("v", [ev_mx], lambda v, c=st(slot, 13):
                         v.tensor_reduce(c, mxc[:, 0:8], AX.XYZW, ALU.max, negate=True),
                         inc=s_ve)
            ev_exp = emit("s", wl(ev_nm, sim_evac_last),
                          lambda s, c=st(slot, 13), a=st(slot, 14):
                          s.activation(sim_bf[:, :], sim_bf[:, :], AF.Exp, bias=c,
                                       accum_out=a), inc=s_se)
            ev_rs = emit("v", wl(ev_exp), lambda v, a=st(slot, 14), co=st(slot, 15):
                         v.reciprocal(co, a), inc=s_ve)
            sim_cons = ev_exp
            if b1_stage < 5:
                slot_free[slot] = ev_rs
                continue
            at_evac = [None] * 4

            def emit_av(gi):
                def bldAV(t, gi=gi):
                    for kk in range(8):
                        kt = gi * 8 + kk
                        for dh in range(2):
                            ins = t.matmul(
                                pAV[:, dh * 512:(dh + 1) * 512],
                                attn_T[gi % 2][:, kk * 128:(kk + 1) * 128],
                                V_km[:, kt * D + dh * 512: kt * D + (dh + 1) * 512],
                                start=(kt == 0), stop=(kt == NT - 1))
                    return ins
                ev = emit("t", wl(at_evac[gi], pAV_free if gi == 0 else None),
                          bldAV, inc=s_mm)
                at_free[gi % 2] = ev
                return ev

            for gi in range(4):
                j = gi % 2
                def bldT(t, j=j, gi=gi):
                    for kk in range(8):
                        kc = gi * 8 + kk
                        ins = t.transpose(pT[j][:, kk * 128:(kk + 1) * 128],
                                          sim_bf[:, kc * 128:(kc + 1) * 128], eye[:, :])
                    return ins
                ev_g = emit("t", wl(ev_exp, pT_free[j]), bldT, inc=s_mm)
                ev_at = emit("v", wl(ev_g, at_free[j]), lambda v, j=j:
                             v.tensor_copy(attn_T[j][:, :], pT[j][:, :]), inc=s_ve)
                pT_free[j] = ev_at
                at_evac[gi] = ev_at
                if gi >= 1:
                    emit_av(gi - 1)
            av_last = emit_av(3)
            sim_cons = ev_g
            xnf_free[i] = av_last
            ev_ave = emit("s", wl(av_last, ev_rs), lambda s, q_i=q_i, r=st(slot, 15):
                          s.activation(AV_all[:, q_i * D:(q_i + 1) * D], pAV[:, :],
                                       AF.Copy, scale=r), inc=s_se)
            pAV_free = ev_ave
            slot_free[slot] = ev_ave

        if debug:
            emit("y", [(s_se, s_se.n)], lambda sy:
                 sy.dma_start(dbg["av"][:, :], AV_all[:, :]), inc=s_out, amt=16)
            emit("y", [(s_out, s_out.n)], lambda sy: sy.nop())
        run_block()
        esB.close()
        esKV.close()
        if last_phase <= 2:
            return nc

        y_fm = sb("y_fm", [128, DC * T], BF16)
        es23 = ExitStack()
        t3_sb = sb("t3_sb", [128, QT * D], F32, stack=es23)
        t1_full = sb("t1_full", [128, 8 * T], BF16, stack=es23)

        es1 = ExitStack()
        w1_fm = sb("w1_fm", [128, DC * D], BF16, side="right", stack=es1)
        b1col = sb("b1col", [128, 8], F32, side="right", stack=es1)
        AVfm = sb("AVfm", [128, DC * 256], BF16, side="right", stack=es1)
        pTb = [es1.enter_context(nc.psum_tensor("pTc0", [128, 1024], BF16)),
               es1.enter_context(nc.psum_tensor("pTc1", [128, 1024], BF16))]
        pt1 = [es1.enter_context(nc.psum_tensor("pt10", [128, 512], F32)),
               es1.enter_context(nc.psum_tensor("pt11", [128, 512], F32))]

        new_streams()
        dma_cast_w(w1, 0, DC, D, w1_fm[:, :])
        ev_w1 = emit("g", [], lambda g:
                     g.dma_start(b1col[:, :], bass.AP(b1, 0, [[1, 128], [128, 8]]),
                                 allow_slow_non_contiguous=True),
                     inc=s_w[0], amt=16)

        pTb_free = [None, None]
        pt1_free = [None, None]
        AVfm_free = None
        for grp in range(8):
            ev_e = None
            for s2 in range(2):
                j = s2
                def bldT(t, grp=grp, s2=s2, j=j):
                    for dc in range(DC):
                        ins = t.transpose(
                            pTb[j][:, dc * 128:(dc + 1) * 128],
                            AV_all[:, (grp * 2 + s2) * D + dc * 128:
                                   (grp * 2 + s2) * D + (dc + 1) * 128], eye[:, :])
                    return ins
                ev_g = emit("t", wl(pTb_free[j], AVfm_free if s2 == 0 else None),
                            bldT, inc=s_mm)
                out_ap = bass.AP(AVfm, s2 * 128, [[DC * 256, 128], [256, DC], [1, 128]])
                ev_e = emit("v", wl(ev_g), lambda v, out_ap=out_ap, j=j:
                            v.tensor_copy(out_ap, pTb[j][:, :]), inc=s_ve)
                pTb_free[j] = ev_e
            for f1c in range(8):
                j = f1c % 2
                def bldM(t, f1c=f1c, j=j):
                    for dc in range(DC):
                        ins = t.matmul(pt1[j][:, 0:256],
                                       w1_fm[:, dc * D + f1c * 128: dc * D + (f1c + 1) * 128],
                                       AVfm[:, dc * 256:(dc + 1) * 256],
                                       start=(dc == 0), stop=(dc == DC - 1))
                    return ins
                ev_m = emit("t", wl(ev_e, ev_w1, pt1_free[j]), bldM, inc=s_mm)
                if f1c == 7:
                    AVfm_free = ev_m
                ev_t1 = emit("s", wl(ev_m), lambda s, grp=grp, f1c=f1c, j=j:
                             s.activation(
                                 t1_full[:, f1c * T + grp * 256: f1c * T + grp * 256 + 256],
                                 pt1[j][:, 0:256], AF.Relu, bias=b1col[:, f1c:f1c + 1]),
                             inc=s_se)
                pt1_free[j] = ev_t1
        if debug:
            emit("y", [(s_se, s_se.n)], lambda sy:
                 sy.dma_start(dbg["t1"][:, :], t1_full[:, :]), inc=s_out, amt=16)
            emit("y", [(s_out, s_out.n)], lambda sy: sy.nop())
        run_block()
        es1.close()
        esAV.close()
        if last_phase <= 3:
            es23.close()
            return nc

        es2 = ExitStack()
        w2q = [sb(f"w2q{j}", [128, DC * 512], BF16, side="right", stack=es2)
               for j in range(2)]
        w3q = [sb(f"w3q{j}", [128, 4 * D], BF16, side="right", stack=es2)
               for j in range(2)]
        b2col = sb("b2col", [128, 32], F32, side="right", stack=es2)
        b3_row = sb("b3_row", [1, D], BF16, side="right", stack=es2)
        t2c = [sb(f"t2c{j}", [128, 256], BF16, side="right", stack=es2)
               for j in range(2)]
        pt2 = [es2.enter_context(nc.psum_tensor(f"pt2{j}", [128, 512], F32))
               for j in range(2)]
        pt3 = [es2.enter_context(nc.psum_tensor(f"pt3{j}", [128, D], F32))
               for j in range(2)]

        new_streams()
        emit("g", [], lambda g:
             g.dma_start(b2col[:, :], bass.AP(b2, 0, [[1, 128], [128, 32]]),
                          allow_slow_non_contiguous=True),
             inc=s_w[0], amt=16)
        ev_b23 = emit("g", [], lambda g:
                      g.dma_start(b3_row[:, :], bass.AP(b3, 0, [[1, 1], [1, D]])),
                      inc=s_w[0], amt=16)

        w_loaded = {}
        w_used = [None, None]

        def emit_wload_s2(hq):
            j = hq % 2
            if w_used[j]:
                emit("g", wl(w_used[j]), lambda g: g.engine_nop())
            dma_cast_w(w2, hq * D * 512, DC, 512, w2q[j][:, :], sem=s_w[j])
            w_loaded[hq] = dma_cast_w(w3, hq * 512 * D, 4, D, w3q[j][:, :], sem=s_w[j])

        emit_wload_s2(0)
        emit_wload_s2(1)

        pt2_free = [None, None]
        t2c_free = [None, None]
        t3_flush = {}
        last_flush = None
        for hq in range(8 if s2_stage > 1 else 0):
            j = hq % 2
            ev_f = None
            for grp in range(8):
                ev_3 = None
                pend2 = None

                def emit_t3(pend2, hq=hq, j=j):
                    jc_, hc_, ev_r_ = pend2
                    wA = wl(ev_r_, last_flush if hc_ == 0 else None)
                    def bldA(t, hq=hq, hc=hc_, jc=jc_, j=j):
                        for s2 in range(2):
                            strt = (hc == 0)
                            if hq == 0 and hc == 0:
                                for dh in range(2):
                                    t.matmul(pt3[s2][:, dh * 512:(dh + 1) * 512],
                                             ones[0:1, :],
                                             b3_row[0:1, dh * 512:(dh + 1) * 512],
                                             start=True, stop=False)
                                strt = False
                            for dh in range(2):
                                ins = t.matmul(
                                    pt3[s2][:, dh * 512:(dh + 1) * 512],
                                    t2c[jc][:, s2 * 128:(s2 + 1) * 128],
                                    w3q[j][:, hc * D + dh * 512: hc * D + (dh + 1) * 512],
                                    start=strt, stop=(hc == 3))
                        return ins
                    return emit("t", wA, bldA, inc=s_mm)

                for hc in range(4):
                    jc = hc % 2
                    def bldM(t, grp=grp, hc=hc, jc=jc, j=j):
                        for dc in range(DC):
                            ins = t.matmul(
                                pt2[jc][:, 0:256],
                                w2q[j][:, dc * 512 + hc * 128: dc * 512 + (hc + 1) * 128],
                                t1_full[:, dc * T + grp * 256: dc * T + grp * 256 + 256],
                                start=(dc == 0), stop=(dc == DC - 1))
                        return ins
                    ev_2 = emit("t", wl(w_loaded[hq], ev_b23, pt2_free[jc]), bldM,
                                inc=s_mm)
                    if pend2 is not None:
                        ev_3 = emit_t3(pend2)
                        t2c_free[pend2[0]] = ev_3
                    ev_r = emit("s", wl(ev_2, t2c_free[jc]),
                                lambda s, hq=hq, hc=hc, jc=jc:
                                s.activation(t2c[jc][:, :], pt2[jc][:, 0:256], AF.Relu,
                                             bias=b2col[:, hq * 4 + hc: hq * 4 + hc + 1]),
                                inc=s_se)
                    pt2_free[jc] = ev_r
                    pend2 = (jc, hc, ev_r)
                ev_3 = emit_t3(pend2)
                t2c_free[pend2[0]] = ev_3
                if s2_stage >= 4:
                    for s2 in range(2):
                        sub = grp * 2 + s2
                        def bldF(v, hq=hq, sub=sub, s2=s2):
                            dst = t3_sb[:, sub * D:(sub + 1) * D]
                            if hq == 0:
                                return v.tensor_copy(dst, pt3[s2][:, :])
                            return v.tensor_tensor(dst, dst, pt3[s2][:, :], ALU.add)
                        ev_f = emit("v", wl(ev_3), bldF, inc=s_ve)
                else:
                    ev_f = ev_3
                t3_flush[grp] = ev_f
                last_flush = ev_f
            w_used[j] = ev_f
            if hq + 2 < 8:
                emit_wload_s2(hq + 2)
        run_block()
        es2.close()
        if last_phase <= 4:
            es23.close()
            return nc

        es3 = ExitStack()
        x1t = [sb(f"x1t{j}", [128, D], F32, side="right", stack=es3)
               for j in range(2)]
        pT3 = [es3.enter_context(nc.psum_tensor("pTd0", [128, 1024], BF16)),
               es3.enter_context(nc.psum_tensor("pTd1", [128, 1024], BF16))]
        junk3 = t1_full[:, 0:1024]

        new_streams()
        x_free = [None, None]
        x1_free = [None, None]
        xnb_free = [None, None]
        pT_free = [None, None]
        slot_free = [None] * 4
        for sub in range(QT):
            i = sub % 2
            slot = sub % 4
            ev_x = emit("y", wl(x_free[i]), lambda sy, sub=sub, i=i:
                        sy.dma_start(xbuf[i][:, :], xq[sub * 128:(sub + 1) * 128, :]),
                        inc=s_x[i], amt=16)
            src = t3_sb[:, sub * D:(sub + 1) * D]
            ev_rstd = ln_stats(slot, src, junk3, wl(slot_free[slot]), wl(slot_free[slot]))
            emit("v", wl(ev_rstd, ev_x, x1_free[i]),
                 lambda v, i=i, src=src, m=st(slot, 2), r=st(slot, 7):
                 v.tensor_scalar(out=x1t[i][:, :], in0=src, scalar1=m, scalar2=r,
                                 op0=ALU.subtract, op1=ALU.mult))
            ev_x1 = emit("v", [], lambda v, i=i:
                         v.tensor_tensor(x1t[i][:, :], x1t[i][:, :], xbuf[i][:, :],
                                         ALU.add), inc=s_ve)
            x_free[i] = ev_x1
            ev_st = emit("y", wl(ev_x1), lambda sy, sub=sub, i=i:
                         sy.dma_start(y[sub * 128:(sub + 1) * 128, :], x1t[i][:, :]),
                         inc=s_st[i], amt=16)
            ev_rstd2 = ln_stats(slot, x1t[i][:, :], junk3, wl(ev_x1), wl(ev_x1))
            ev_ap = ln_apply(slot, x1t[i][:, :], xnb[i][:, :],
                             wl(ev_rstd2, xnb_free[i]))
            x1_free[i] = [ev_ap, ev_rstd2, ev_st]
            ev_tp = transposes8(xnb[i][:, :], pT3[i], wl(ev_ap, pT_free[i]))
            xnb_free[i] = ev_tp
            out_ap = bass.AP(y_fm, sub * 128, [[DC * T, 128], [T, DC], [1, 128]])
            ev_yf = emit("v", wl(ev_tp), lambda v, out_ap=out_ap, i=i:
                         v.tensor_copy(out_ap, pT3[i][:, :]), inc=s_ve)
            pT_free[i] = ev_yf
            slot_free[slot] = ev_yf
        if debug:
            emit("y", [(s_ve, s_ve.n)], lambda sy:
                 sy.dma_start(dbg["t3"][:, :], t3_sb[:, :]), inc=s_out, amt=16)
            emit("y", [], lambda sy:
                 sy.dma_start(dbg["yfm"][:, :], y_fm[:, :]), inc=s_out, amt=16)
            emit("y", [(s_out, s_out.n)], lambda sy: sy.nop())
        run_block()
        es3.close()
        es23.close()
        if last_phase <= 5:
            return nc

        esC = ExitStack()
        wgq = [sb(f"wgq{j}", [128, DC * 1024], BF16, side="right", stack=esC)
               for j in range(2)]
        wuq = [sb(f"wuq{j}", [128, DC * 1024], BF16, side="right", stack=esC)
               for j in range(2)]
        wdq = [sb(f"wdq{j}", [128, 8 * D], BF16, side="right", stack=esC)
               for j in range(2)]
        h_bf = [sb(f"hbf{j}", [128, 256], BF16, side="right", stack=esC)
                for j in range(2)]
        gsil = [sb(f"gsil{j}", [128, 256], F32, side="right", stack=esC)
                for j in range(2)]
        x2t = [sb(f"x2t{j}", [128, D], F32, side="right", stack=esC)
               for j in range(2)]
        stg = [sb(f"stg{j}", [128, D], F32, side="right", stack=esC)
               for j in range(2)]
        pacc = [esC.enter_context(nc.psum_tensor(f"pacc{j}", [128, D], F32))
                for j in range(2)]
        pg = [esC.enter_context(nc.psum_tensor(f"pg{j}", [128, 512], F32))
              for j in range(2)]
        pu = [esC.enter_context(nc.psum_tensor(f"pu{j}", [128, 512], F32))
              for j in range(2)]
        junkC = y_fm[:, 0:1024]

        new_streams()
        QBLK = D * 1024
        w_loaded = {}
        w_used = [None, None]

        def emit_wload_c(blk_idx):
            j = blk_idx % 2
            if w_used[j]:
                emit("g", wl(w_used[j]), lambda g: g.engine_nop())
            off = blk_idx * QBLK
            dma_cast_w(wg, off, DC, 1024, wgq[j][:, :], sem=s_w[j])
            dma_cast_w(wu, off, DC, 1024, wuq[j][:, :], sem=s_w[j])
            w_loaded[blk_idx] = dma_cast_w(wd, off, 8, D, wdq[j][:, :], sem=s_w[j])

        emit_wload_c(0)
        emit_wload_c(1)

        pg_free = [None, None]
        pu_free = [None, None]
        h_free = [None, None]
        gs_free = [None, None]
        stg_free = [None, None]
        acc_dma = [None] * QT
        pacc_free = [None, None]
        for blk_idx in range(32):
            j = blk_idx % 2
            ev_a = None
            for TG in range(8):
                pend = None

                def emit_acc(pend, j=j):
                    jc_, fc_, ev_h_ = pend
                    wA = wl(ev_h_, pacc_free if fc_ == 0 else None)
                    def bldA(t, j=j, jc=jc_, fc=fc_):
                        for s2 in range(2):
                            for dh in range(2):
                                ins = t.matmul(
                                    pacc[s2][:, dh * 512:(dh + 1) * 512],
                                    h_bf[jc][:, s2 * 128:(s2 + 1) * 128],
                                    wdq[j][:, fc * D + dh * 512: fc * D + (dh + 1) * 512],
                                    start=(fc == 0), stop=(fc == 7))
                        return ins
                    return emit("t", wA, bldA, inc=s_mm)

                for fc in range(8):
                    jc = fc % 2
                    def bldG(t, j=j, jc=jc, TG=TG, fc=fc):
                        for dc in range(DC):
                            ins = t.matmul(
                                pg[jc][:, 0:256],
                                wgq[j][:, dc * 1024 + fc * 128: dc * 1024 + (fc + 1) * 128],
                                y_fm[:, dc * T + TG * 256: dc * T + TG * 256 + 256],
                                start=(dc == 0), stop=(dc == DC - 1))
                        return ins
                    ev_gg = emit("t", wl(w_loaded[blk_idx], pg_free[jc]), bldG, inc=s_mm)
                    def bldU(t, j=j, jc=jc, TG=TG, fc=fc):
                        for dc in range(DC):
                            ins = t.matmul(
                                pu[jc][:, 0:256],
                                wuq[j][:, dc * 1024 + fc * 128: dc * 1024 + (fc + 1) * 128],
                                y_fm[:, dc * T + TG * 256: dc * T + TG * 256 + 256],
                                start=(dc == 0), stop=(dc == DC - 1))
                        return ins
                    ev_uu = emit("t", wl(pu_free[jc]), bldU, inc=s_mm)
                    if pend is not None:
                        ev_a = emit_acc(pend)
                        h_free[pend[0]] = ev_a
                    ev_si = emit("s", wl(ev_gg, gs_free[jc]), lambda s, jc=jc:
                                 s.activation(gsil[jc][:, :], pg[jc][:, 0:256], AF.Silu),
                                 inc=s_se)
                    pg_free[jc] = ev_si
                    ev_h = emit("v", wl(ev_si, ev_uu, h_free[jc]), lambda v, jc=jc:
                                v.tensor_tensor(h_bf[jc][:, :], gsil[jc][:, :],
                                                pu[jc][:, 0:256], ALU.mult), inc=s_ve)
                    pu_free[jc] = ev_h
                    gs_free[jc] = ev_h
                    pend = (jc, fc, ev_h)
                ev_a = emit_acc(pend)
                h_free[pend[0]] = ev_a
                evs = []
                for s2 in range(2):
                    sub = TG * 2 + s2
                    js = sub % 2
                    ev_ev = emit("v", wl(ev_a, stg_free[js]),
                                 lambda v, s2=s2, js=js:
                                 v.tensor_copy(stg[js][:, :], pacc[s2][:, :]),
                                 inc=s_ve)
                    evs.append(ev_ev)
                    op = ALU.bypass if blk_idx == 0 else ALU.add
                    ev_d = emit("g", wl(ev_ev, acc_dma[sub]),
                                lambda g, sub=sub, js=js, op=op:
                                g.dma_start(acc_s[sub * 128:(sub + 1) * 128, :],
                                            stg[js][:, :], accum_op=op),
                                inc=s_fl[js], amt=16)
                    acc_dma[sub] = ev_d
                    stg_free[js] = ev_d
                pacc_free = evs
            w_used[j] = ev_a
            if blk_idx + 2 < 32:
                emit_wload_c(blk_idx + 2)

        all_mm = (s_mm, s_mm.n)
        n_stash = [(s_st[0], s_st[0].n), (s_st[1], s_st[1].n)]
        x_free = [None, None]
        x2_free = [None, None]
        stg2_free = [stg_free[0], stg_free[1]]
        slot_free = [None] * 4
        for sub in range(QT):
            i = sub % 2
            slot = sub % 4
            emit("y", wl(acc_dma[sub], x2_free[i]),
                 lambda sy, sub=sub, i=i:
                 sy.dma_start(x2t[i][:, :], acc_s[sub * 128:(sub + 1) * 128, :]),
                 inc=s_x[i], amt=16)
            ev_x = emit("y", wl(n_stash, x_free[i]), lambda sy, sub=sub, i=i:
                        sy.dma_start(xbuf[i][:, :], y[sub * 128:(sub + 1) * 128, :]),
                        inc=s_x[i], amt=16)
            ev_acc = ev_x
            ev_rstd = ln_stats(slot, x2t[i][:, :], junkC,
                               wl(slot_free[slot], ev_acc),
                               wl(slot_free[slot], ev_acc, all_mm))
            emit("v", wl(ev_rstd, ev_x, stg2_free[i]),
                 lambda v, i=i, m=st(slot, 2), r=st(slot, 7):
                 v.tensor_scalar(out=stg[i][:, :], in0=x2t[i][:, :], scalar1=m,
                                 scalar2=r, op0=ALU.subtract, op1=ALU.mult))
            ev_x2 = emit("v", [], lambda v, i=i:
                         v.tensor_tensor(stg[i][:, :], stg[i][:, :], xbuf[i][:, :],
                                         ALU.add), inc=s_ve)
            x_free[i] = ev_x2
            x2_free[i] = ev_x2
            slot_free[slot] = ev_x2
            ev_o = emit("y", wl(ev_x2), lambda sy, sub=sub, i=i:
                        sy.dma_start(y[sub * 128:(sub + 1) * 128, :], stg[i][:, :]),
                        inc=s_st[i], amt=16)
            stg2_free[i] = ev_o
        emit("y", [(s_st[0], s_st[0].n), (s_st[1], s_st[1].n),
                   (s_out, s_out.n)], lambda sy: sy.nop())
        run_block()
        esC.close()

    return nc


_NC_CACHE = {}


def _get_nc(debug=False):
    key = bool(debug)
    if key not in _NC_CACHE:
        _NC_CACHE[key] = build_program(debug=debug)
    return _NC_CACHE[key]


def make_in_maps(inputs):
    X = np.asarray(inputs["L0_attn_x"], dtype=np.float32)
    Wq = np.asarray(inputs["L0_attn_Wq"], dtype=np.float32)
    Wk = np.asarray(inputs["L0_attn_Wk"], dtype=np.float32)
    Wv = np.asarray(inputs["L0_attn_Wv"], dtype=np.float32)
    W1 = np.asarray(inputs["L0_attn_W1"], dtype=np.float32)
    W2 = np.asarray(inputs["L0_attn_W2"], dtype=np.float32)
    W3 = np.asarray(inputs["L0_attn_W3"], dtype=np.float32)
    Wg = np.asarray(inputs["L0_ffn_Wg"], dtype=np.float32)
    Wu = np.asarray(inputs["L0_ffn_Wu"], dtype=np.float32)
    Wd = np.asarray(inputs["L0_ffn_Wd"], dtype=np.float32)

    shared = {
        "wq": np.ascontiguousarray(Wq.T),
        "wk": np.ascontiguousarray(Wk.T),
        "wv": np.ascontiguousarray(Wv.T),
        "w1": np.ascontiguousarray(W1.T),
        "w2": np.ascontiguousarray(W2.T.reshape(D, 8, 512).transpose(1, 0, 2)),
        "w3": np.ascontiguousarray(W3.T.reshape(8, 512, D)),
        "bq": np.asarray(inputs["L0_attn_bq"], np.float32),
        "bk": np.asarray(inputs["L0_attn_bk"], np.float32),
        "bv": np.asarray(inputs["L0_attn_bv"], np.float32),
        "b1": np.asarray(inputs["L0_attn_b1"], np.float32),
        "b2": np.asarray(inputs["L0_attn_b2"], np.float32),
        "b3": np.asarray(inputs["L0_attn_b3"], np.float32),
        "wg": np.ascontiguousarray(Wg.reshape(E, D, 4, 1024).transpose(0, 2, 1, 3)),
        "wu": np.ascontiguousarray(Wu.reshape(E, D, 4, 1024).transpose(0, 2, 1, 3)),
        "wd": np.ascontiguousarray(Wd.reshape(E, 4, 1024, D)),
    }
    in_maps = []
    for c in range(8):
        b, hh = divmod(c, 2)
        m = dict(shared)
        m["xs"] = np.ascontiguousarray(X[b])
        m["xq"] = np.ascontiguousarray(X[b, hh * T:(hh + 1) * T])
        in_maps.append(m)
    return in_maps


def kernel(**inputs):
    nc = _get_nc(debug=False)
    in_maps = make_in_maps(inputs)
    res = run_bass_kernel_spmd(nc, in_maps, core_ids=list(range(8)))
    out = np.empty((B, S, D), dtype=np.float32)
    for c in range(8):
        b, hh = divmod(c, 2)
        out[b, hh * T:(hh + 1) * T] = res.results[c]["y"]
    return out
